# version 01
# baseline (speedup 1.0000x reference)
import numpy as np
from contextlib import ExitStack

import concourse.bass as bass
import concourse.mybir as mybir
from concourse.bass_utils import run_bass_kernel_spmd

F32 = mybir.dt.float32
BF16 = mybir.dt.bfloat16
I32 = mybir.dt.int32
AF = mybir.ActivationFunctionType
ALU = mybir.AluOpType
AX = mybir.AxisListType

T = 2048
D = 1024
NFT = 8
NBLK = 22
DEPTH = 6
NEG = -1.0e30
ENGS = ("sync", "gpsimd", "tensor", "vector", "scalar")


class Sem:
    def __init__(self, nc, ctx, name):
        self.h = ctx.enter_context(nc.semaphore(name))
        self.n = 0

    def bump(self, k=1):
        self.n += k
        return (self.h, self.n)


class Ph:
    def __init__(self, name):
        self.name = name
        self.q = {e: [] for e in ENGS}

    def emit(self, nc):
        with nc.Block(self.name) as blk:
            for e in ENGS:
                fns = self.q[e]
                if fns:
                    def runner(eng, fns=fns):
                        for f in fns:
                            f(eng)
                    getattr(blk, e)(runner)


class _EngProxy:

    def __init__(self, eng):
        object.__setattr__(self, "_e", eng)
        object.__setattr__(self, "last", None)

    def __getattr__(self, k):
        a = getattr(self._e, k)
        if callable(a):
            def w(*args, **kw):
                r = a(*args, **kw)
                if hasattr(r, "then_inc"):
                    object.__setattr__(self, "last", r)
                return r
            return w
        return a


def _dedup(waits):
    best = {}
    for w in waits:
        if w is None:
            continue
        h, v = w
        k = id(h)
        if k not in best or best[k][1] < v:
            best[k] = (h, v)
    return tuple(best.values())


class B:
    def __init__(self, n_layers=DEPTH, dbg=()):
        self.n_layers = n_layers
        self.dbg_req = set(dbg)
        nc = bass.Bass("TRN2", target_bir_lowering=False,
                       detect_race_conditions=False)
        self.nc = nc
        self.ctx = ExitStack()
        self.phases = []
        self._uid = 0

        dt = nc.dram_tensor
        self.i = {}

        def inp(name, shape):
            self.i[name] = dt(name, shape, F32, kind="ExternalInput").ap()
        inp("xT", [D, T]); inp("wqkvT", [D, 3 * D]); inp("woutT", [D, D])
        inp("wentR", [128, 8 * 128]); inp("bqkvP", [128, 24]); inp("boutP", [128, 8])
        inp("bentR", [128, 1]); inp("lngP", [128, 8]); inp("lnbP", [128, 8])
        inp("fWqkv", [DEPTH, D, 3 * D]); inp("fWo", [DEPTH, D, D])
        inp("fW1blk", [DEPTH, NBLK, D, 256]); inp("fW2", [DEPTH, NBLK * 128, D])
        inp("fb1aP", [128, DEPTH * NBLK]); inp("fb1gP", [128, DEPTH * NBLK])
        inp("fb2P", [128, DEPTH * 8]); inp("flngP", [128, DEPTH * 8])
        inp("flnbP", [128, DEPTH * 8]); inp("fqsP", [128, DEPTH])
        inp("fksP", [128, DEPTH])
        inp("identC", [128, 128]); inp("protC", [128, 128])
        inp("maskloC", [128, 128]); inp("mask3C", [128, 384])
        inp("indrC", [128, 128])
        inp("ones1C", [128, 1]); inp("onesK1C", [1, 128]); inp("onesPC", [128, 128])
        inp("tgridC", [1, T]); inp("jgridC", [1, 128]); inp("sgnC", [128, 1])
        inp("c4C", [1, 4])
        self.o_y = dt("yT", [D, T], F32, kind="ExternalOutput").ap()
        self.d_acc = dt("accS", [D, T], F32, kind="Internal").ap()
        self.d_xn = dt("xnS", [D, T], F32, kind="Internal").ap()
        self.d_tab = dt("tabS", [4 * 128, T], BF16, kind="Internal").ap()
        self.dbg_outs = {}

        self._dbg_last_dma = None
        c = self.ctx
        self._dbg_scr = (self.ctx.enter_context(
            nc.sbuf_tensor("dbgscr", [128, 1024], F32))
            if self.dbg_req else None)
        self.Tm = Sem(nc, c, "semT")
        self.V = Sem(nc, c, "semV")
        self.S = Sem(nc, c, "semS")
        self.chan = {}
        for name in ("x", "const", "xnst0", "xnst1", "xnld0", "xnld1",
                     "xnfull", "tabld", "accst", "accld0", "accld1",
                     "y0", "y1", "dbg", "gconst", "gwhp0", "gwhp1", "gwos",
                     "gw10", "gw11", "gw20", "gw21", "gtabst"):
            self.chan[name] = Sem(nc, c, "c_" + name)

    def uid(self):
        self._uid += 1
        return self._uid

    def phase(self, name):
        ph = Ph(name)
        self.phases.append(ph)
        return ph

    @staticmethod
    def _do(ph, eng, fn, waits, inc_sem, inc_by):
        w = _dedup(waits)

        def f(e):
            for (h, v) in w:
                e.wait_ge(h, v)
            p = _EngProxy(e)
            ins = fn(p)
            if inc_sem is not None:
                if ins is None or not hasattr(ins, "then_inc"):
                    ins = p.last
                assert ins is not None, "op emitted no instruction"
                ins.then_inc(inc_sem, inc_by)
        ph.q[eng].append(f)

    def t(self, ph, fn, waits=(), inc=True):
        self._do(ph, "tensor", fn, waits, self.Tm.h if inc else None, 1)
        return self.Tm.bump() if inc else None

    def v(self, ph, fn, waits=(), inc=True):
        self._do(ph, "vector", fn, waits, self.V.h if inc else None, 1)
        return self.V.bump() if inc else None

    def s(self, ph, fn, waits=(), inc=True):
        self._do(ph, "scalar", fn, waits, self.S.h if inc else None, 1)
        return self.S.bump() if inc else None

    def dma(self, ph, fn, chan, waits=(), queue="sync"):
        sem = self.chan[chan]
        self._do(ph, queue, fn, waits, sem.h, 16)
        return sem.bump(16)

    def sb(self, name, shape, dtype, stack=None, side=None):
        kw = {} if side is None else {"side": side}
        return (stack or self.ctx).enter_context(
            self.nc.sbuf_tensor(name, shape, dtype, **kw))

    def ps(self, name, shape, dtype=F32, stack=None):
        return (stack or self.ctx).enter_context(
            self.nc.psum_tensor(name, shape, dtype))

    def dump(self, name, ap, shape):
        if name not in self.dbg_req:
            return
        assert shape[0] <= 128 and int(np.prod(shape[1:])) <= 1024
        t = self.nc.dram_tensor("dbg_" + name, shape, F32,
                                kind="ExternalOutput").ap()
        self.dbg_outs[name] = shape
        scr = self._dbg_scr
        n = int(np.prod(shape[1:]))
        ph = self.phase("dbg_" + name)
        wv = [self._dbg_last_dma] if self._dbg_last_dma else []
        ev = self.v(ph, lambda e, a=ap, s=scr: e.tensor_copy(
            s[0:shape[0], 0:n], a), waits=wv)
        self._dbg_last_dma = self.dma(
            ph, lambda e, t=t, s=scr: e.dma_start(
                out=t, in_=s[0:shape[0], 0:n]), "dbg", waits=[ev])

    def finish(self):
        for ph in self.phases:
            ph.emit(self.nc)
        with self.nc.Block("fin") as blk:
            tot = [(sm.h, sm.n) for sm in self.chan.values() if sm.n > 0]

            def f(e, tot=tot):
                for h, v in tot:
                    e.wait_ge(h, v)
            blk.sync(f)
        self.ctx.close()
        return self.nc



def ln_stats_bcast(b, ph, XSRC, src_is_bf16, XB, XSQ, ONESPB, A, C, MUR, eps,
                   ev_first, ev_each=None, extra=(), raw=False):
    u = b.uid()
    stk = ExitStack()
    PSS = b.ps(f"lnb{u}_s", [128, 1024], F32, stack=stk)
    PSQ2 = b.ps(f"lnb{u}_q", [128, 1024], F32, stack=stk)
    PSS2 = b.ps(f"lnb{u}_s2", [128, 1024], F32, stack=stk)
    PSQ3 = b.ps(f"lnb{u}_q2", [128, 1024], F32, stack=stk)
    import os as _o3
    if _o3.environ.get("PSSWAP", "0") == "1":
        PSS, PSQ2 = PSQ2, PSS
        PSS2, PSQ3 = PSQ3, PSS2
    ev_acc = None
    for ft in range(NFT):
        wv = ([ev_first] if ft == 0 else [ev_acc]) + list(extra)
        if ev_each is not None and ev_each[ft] is not None:
            wv.append(ev_each[ft])
        if src_is_bf16:
            SRC = XSRC[:, ft * T:(ft + 1) * T]
            ev_c = None
            ev_sq = b.s(ph, lambda e, SRC=SRC: e.activation(
                XSQ[:], SRC, AF.Square), waits=wv)
        else:
            ev_c = b.s(ph, lambda e, ft=ft: e.activation(
                XB[:], XSRC[:, ft * T:(ft + 1) * T], AF.Copy), waits=wv)
            SRC = XB[:]
            ev_sq = b.v(ph, lambda e: e.tensor_mul(XSQ[:], XB[:], XB[:]),
                        waits=[ev_c] + wv)

        def tmm(e, SRC=SRC):
            for h in range(2):
                PP = PSS if h == 0 else PSS2
                for c in range(2):
                    e.matmul(PP[:, c * 512:(c + 1) * 512], ONESPB[:, :],
                             SRC[:, h * 1024 + c * 512:h * 1024 + (c + 1) * 512]
                             if src_is_bf16 else
                             XB[:, h * 1024 + c * 512:h * 1024 + (c + 1) * 512],
                             start=True, stop=True)
            for h in range(2):
                PP = PSQ2 if h == 0 else PSQ3
                for c in range(2):
                    e.matmul(PP[:, c * 512:(c + 1) * 512], ONESPB[:, :],
                             XSQ[:, h * 1024 + c * 512:h * 1024 + (c + 1) * 512],
                             start=True, stop=True)
        ev_mm = b.t(ph, tmm, waits=[ev_sq] + list(extra))

        def vacc(e, ft=ft):
            if ft == 0:
                e.tensor_copy(MUR[:, 0:1024], PSS[:])
                e.tensor_copy(MUR[:, 1024:T], PSS2[:])
                e.tensor_copy(C[:, 0:1024], PSQ2[:])
                e.tensor_copy(C[:, 1024:T], PSQ3[:])
            else:
                e.tensor_add(MUR[:, 0:1024], MUR[:, 0:1024], PSS[:])
                e.tensor_add(MUR[:, 1024:T], MUR[:, 1024:T], PSS2[:])
                e.tensor_add(C[:, 0:1024], C[:, 0:1024], PSQ2[:])
                e.tensor_add(C[:, 1024:T], C[:, 1024:T], PSQ3[:])
        ev_acc = b.v(ph, vacc, waits=[ev_mm])
    stk.close()

    def vrows(e):
        e.tensor_scalar_mul(MUR[:], MUR[:], 1.0 / 1024.0)
        e.tensor_scalar_mul(C[:], C[:], 1.0 / 1024.0)
        e.tensor_mul(A[:], MUR[:], MUR[:])
        e.tensor_sub(A[:], C[:], A[:])
        e.tensor_scalar_add(A[:], A[:], eps)
    ev = b.v(ph, vrows, waits=[ev_acc])
    if raw:
        return ev

    def srows(e):
        e.activation(A[:], A[:], AF.Ln)
        e.activation(A[:], A[:], AF.Exp, scale=-0.5)
    ev = b.s(ph, srows, waits=[ev])

    def vc(e):
        e.tensor_mul(C[:], MUR[:], A[:])
        e.tensor_scalar_mul(C[:], C[:], -1.0)
    ev = b.v(ph, vc, waits=[ev])
    return ev


def shift_cast(b, ph, dst, XN, src_events):
    for ft in range(NFT):
        if ft < 4:
            b.s(ph, lambda e, ft=ft: e.activation(
                dst[:, ft * T:(ft + 1) * T], XN[:, ft * T:(ft + 1) * T], AF.Copy),
                waits=src_events if ft == 0 else (), inc=False)
        else:
            b.s(ph, lambda e, ft=ft: e.activation(
                dst[:, ft * T + 1:(ft + 1) * T],
                XN[:, ft * T:(ft + 1) * T - 1], AF.Copy),
                waits=src_events if ft == 0 else (), inc=False)
            b.v(ph, lambda e, ft=ft: e.memset(dst[:, ft * T:ft * T + 1], 0.0),
                inc=False)
    ev_s = b.s(ph, lambda e: e.nop())
    ev_v = b.v(ph, lambda e: e.engine_nop())
    return [ev_s, ev_v]


def wo_project(b, ph, ATM, AFM, WSTAGE, XN_or_XR, IDENT, PT, ev_src, g_pt,
               epilogue):
    ev_tr = None
    for kc in range(8):
        for fill in range(2):
            wv = list(ev_src)
            if g_pt[0] is not None:
                wv.append(g_pt[0])

            def ttr(e, kc=kc, fill=fill):
                for j in range(8):
                    tt = fill * 8 + j
                    e.transpose(PT[:, j * 128:(j + 1) * 128],
                                ATM[:, tt * 1024 + kc * 128:
                                    tt * 1024 + (kc + 1) * 128], IDENT[:])
            ev_t1 = b.t(ph, ttr, waits=wv)
            ev_tr = b.v(ph, lambda e, kc=kc, fill=fill: e.tensor_copy(
                AFM[:, kc * T + fill * 1024:kc * T + (fill + 1) * 1024], PT[:]),
                waits=[ev_t1])
            g_pt[0] = ev_tr
    stk = ExitStack()
    PW = [b.ps(f"wo{b.uid()}_{j}", [128, 1024], F32, stack=stk) for j in range(2)]
    g_pw = [None, None]
    ev_last = None
    for half in range(2):
        for mc in range(8):
            par = mc % 2
            wv = [ev_tr] + list(ev_src)
            if g_pw[par] is not None:
                wv.append(g_pw[par])

            def tw(e, mc=mc, par=par, half=half):
                for nt in range(2):
                    for kc in range(8):
                        e.matmul(PW[par][:, nt * 512:(nt + 1) * 512],
                                 WSTAGE[:, kc * 1024 + mc * 128:
                                        kc * 1024 + (mc + 1) * 128],
                                 AFM[:, kc * T + half * 1024 + nt * 512:
                                     kc * T + half * 1024 + (nt + 1) * 512],
                                 start=(kc == 0), stop=(kc == 7))
            ev_m = b.t(ph, tw, waits=wv)
            ev_last = epilogue(mc, half, PW[par], ev_m)
            g_pw[par] = ev_last
    stk.close()
    return ev_last


def build(n_layers=DEPTH, dbg=(), upto=None, skip_outer=False):
    b = B(n_layers, dbg)
    nc, i = b.nc, b.i
    OPEN = []

    IDENT = b.sb("IDENT", [128, 128], BF16)
    PROT = b.sb("PROT", [128, 128], BF16)
    MASKLO = b.sb("MASKLO", [128, 128], BF16)
    MASK3 = b.sb("MASK3", [128, 384], BF16)
    INDR = b.sb("INDR", [128, 128], BF16)
    ONES1B = b.sb("ONES1B", [128, 1], BF16)
    ONESPB = b.sb("ONESPB", [128, 128], BF16)
    ONESK1F = b.sb("ONESK1F", [1, 128], F32)
    ONESP = b.sb("ONESP", [128, 128], F32)
    JG = b.sb("JG", [1, 128], F32)
    SGN = b.sb("SGN", [128, 1], F32)
    C4 = b.sb("C4", [1, 4], F32)
    BQKV = b.sb("BQKV", [128, 24], F32)
    BOUT01 = b.sb("BOUT01", [128, 8], F32)
    BENT = b.sb("BENT", [128, 1], F32)
    LNG = b.sb("LNG", [128, 8], F32)
    LNB = b.sb("LNB", [128, 8], F32)
    WENT = b.sb("WENT", [128, 8 * 128], BF16)
    FB1A = b.sb("FB1A", [128, DEPTH * NBLK], F32)
    FB1G = b.sb("FB1G", [128, DEPTH * NBLK], F32)
    FB2 = b.sb("FB2", [128, DEPTH * 8], F32)
    FLNG = b.sb("FLNG", [128, DEPTH * 8], F32)
    FLNB = b.sb("FLNB", [128, DEPTH * 8], F32)
    FQS = b.sb("FQS", [128, DEPTH], F32)
    FQS8 = b.sb("FQS8", [128, DEPTH], F32)
    BQ125 = b.sb("BQ125", [128, 8], F32)
    FKS = b.sb("FKS", [128, DEPTH], F32)
    SMALL = b.sb("SMALL", [128, 32], F32)
    ctxR = ExitStack()
    OPEN.append(ctxR)
    XR = b.sb("XR", [128, NFT * T], F32, stack=ctxR, side="right")

    ph = b.phase("init")
    ev_x = b.dma(ph, lambda e: e.dma_start(
        out=XR[:].rearrange("p (ft t) -> p ft t", ft=NFT),
        in_=i["xT"].rearrange("(ft p) t -> p ft t", p=128)), "x")
    ev_cb = None
    for dst, src in ((IDENT, "identC"), (PROT, "protC"), (MASKLO, "maskloC"),
                     (MASK3, "mask3C"), (INDR, "indrC"), (ONESPB, "onesPC"),
                     (ONES1B, "ones1C"), (WENT, "wentR")):
        ev_cb = b.dma(ph, lambda e, d=dst, s=i[src]: e.dma_start(out=d[:], in_=s),
                      "gconst", queue="gpsimd")
    ev_cf = None
    BOUTF = SMALL
    for dst, src in ((ONESK1F, "onesK1C"),
                     (JG, "jgridC"), (SGN, "sgnC"), (C4, "c4C"), (BQKV, "bqkvP"),
                     (BENT, "bentR"), (LNG, "lngP"), (LNB, "lnbP"),
                     (FB1A, "fb1aP"), (FB1G, "fb1gP"), (FB2, "fb2P"),
                     (FLNG, "flngP"), (FLNB, "flnbP"), (FQS, "fqsP"),
                     (FKS, "fksP")):
        ev_cf = b.dma(ph, lambda e, d=dst, s=i[src]: e.dma_start(out=d[:], in_=s),
                      "const")
    ev_bo = b.dma(ph, lambda e: e.dma_start(out=SMALL[:, 8:16], in_=i["boutP"]),
                  "const")
    ev_cb = (b.chan["gconst"].h, b.chan["gconst"].n)
    ev_cf = (b.chan["const"].h, b.chan["const"].n)
    b.ev_consts = [ev_cb, ev_cf]
    ev_b01 = b.v(ph, lambda e: e.tensor_scalar_mul(BOUT01[:], SMALL[:, 8:16], 0.1),
                 waits=[ev_bo])
    import os as _os
    WARM = _os.environ.get("WARM", "0") == "1"

    def warmup(e):
        for f in (AF.Exp, AF.Ln, AF.Sigmoid, AF.Gelu, AF.Sin, AF.Arctan):
            e.preload_act_table(f)
            e.activation(SMALL[:, 16:17], SMALL[:, 8:9], f) \
                if f not in (AF.Ln,) else None
    if WARM:
        b.s(ph, warmup, waits=[ev_bo], inc=False)
    ev_fqs8 = b.v(ph, lambda e: e.tensor_scalar_mul(FQS8[:], FQS[:], 8.0),
                  waits=[ev_cf])
    ev_bq125 = b.v(ph, lambda e: e.tensor_scalar_mul(
        BQ125[:], BQKV[:, 0:8], 1.25), waits=[ev_cf])
    b.ev_consts = b.ev_consts + [ev_fqs8, ev_bq125, ev_b01]

    def stop_here(tag):
        if upto != tag:
            return False
        for st_ in reversed(OPEN):
            st_.close()
        OPEN.clear()
        ph2 = b.phase("stub_final")
        b.dma(ph2, lambda e: e.dma_start(
            out=b.o_y[0:128, 0:24], in_=BQKV[:]), "y0")
        return True

    if stop_here("init"):
        return b

    ctx1 = ExitStack()
    OPEN.append(ctx1)
    XN2B = b.sb("XN2B", [128, NFT * T], BF16, stack=ctx1)
    GB = b.sb("GB", [128, T], BF16, stack=ctx1)
    ctx1b = ExitStack()
    OPEN.append(ctx1b)
    XB1 = b.sb("p1_xb1", [128, T], BF16, stack=ctx1b)
    XB2 = b.sb("p1_xb2", [128, T], BF16, stack=ctx1b)
    SC_A = b.sb("p1_a", [128, T], F32, stack=ctx1b)
    SC_C = b.sb("p1_c", [128, T], F32, stack=ctx1b)
    SC_A2 = b.sb("p1_a2", [128, T], F32, stack=ctx1b)
    SC_C2 = b.sb("p1_c2", [128, T], F32, stack=ctx1b)
    MUR = b.sb("p1_mur", [128, T], F32, stack=ctx1b)
    XT1 = b.sb("p1_xt", [128, T], F32, stack=ctx1b)
    XT2 = b.sb("p1_xt2", [128, T], F32, stack=ctx1b)

    ph = b.phase("xn_stats")
    ev_rows = ln_stats_bcast(b, ph, XR, False, XB1, XB2, ONESPB,
                             SC_A, SC_C, MUR, 1e-5, ev_x,
                             extra=b.ev_consts)
    if stop_here("stats"):
        return b

    ph = b.phase("xn_tiles")
    stk1 = ExitStack()
    PSS = b.ps("xn2st_s", [128, 1024], F32, stack=stk1)
    PSS2 = b.ps("xn2st_s2", [128, 1024], F32, stack=stk1)
    PSQ2 = b.ps("xn2st_q", [128, 1024], F32, stack=stk1)
    PSQ3 = b.ps("xn2st_q2", [128, 1024], F32, stack=stk1)
    ev_acc2 = None
    ev_st = None
    for ft in range(NFT):
        BUF = XT1 if ft % 2 == 0 else XT2
        wv = [ev_rows]
        if ev_acc2 is not None:
            wv.append(ev_acc2)
        if ev_st is not None and ft >= 2:
            wv.append(ev_prev_st[ft % 2])

        def vt(e, ft=ft, BUF=BUF):
            e.tensor_mul(BUF[:], XR[:, ft * T:(ft + 1) * T], SC_A[:])
            e.tensor_add(BUF[:], BUF[:], SC_C[:])
        ev_t = b.v(ph, vt, waits=wv)
        ev_st = b.dma(ph, lambda e, ft=ft, BUF=BUF: e.dma_start(
            out=b.d_xn.rearrange("(ft p) t -> p ft t", p=128)[:, ft, :],
            in_=BUF[:]), f"xnst{ft % 2}", waits=[ev_t])
        if ft == 0:
            ev_prev_st = [None, None]
        ev_prev_st[ft % 2] = ev_st
        ev_c = b.s(ph, lambda e, BUF=BUF: e.activation(XB1[:], BUF[:], AF.Copy),
                   waits=[ev_t])
        ev_sq = b.v(ph, lambda e: e.tensor_mul(XB2[:], XB1[:], XB1[:]),
                    waits=[ev_c])

        def tmm2(e):
            for h in range(2):
                PP = PSS if h == 0 else PSS2
                for c in range(2):
                    e.matmul(PP[:, c * 512:(c + 1) * 512], ONESPB[:, :],
                             XB1[:, h * 1024 + c * 512:h * 1024 + (c + 1) * 512],
                             start=True, stop=True)
            for h in range(2):
                PP = PSQ2 if h == 0 else PSQ3
                for c in range(2):
                    e.matmul(PP[:, c * 512:(c + 1) * 512], ONESPB[:, :],
                             XB2[:, h * 1024 + c * 512:h * 1024 + (c + 1) * 512],
                             start=True, stop=True)
        ev_mm2 = b.t(ph, tmm2, waits=[ev_sq] + b.ev_consts)

        def vacc2(e, ft=ft):
            if ft == 0:
                e.tensor_copy(SC_A2[:, 0:1024], PSS[:])
                e.tensor_copy(SC_A2[:, 1024:T], PSS2[:])
                e.tensor_copy(SC_C2[:, 0:1024], PSQ2[:])
                e.tensor_copy(SC_C2[:, 1024:T], PSQ3[:])
            else:
                e.tensor_add(SC_A2[:, 0:1024], SC_A2[:, 0:1024], PSS[:])
                e.tensor_add(SC_A2[:, 1024:T], SC_A2[:, 1024:T], PSS2[:])
                e.tensor_add(SC_C2[:, 0:1024], SC_C2[:, 0:1024], PSQ2[:])
                e.tensor_add(SC_C2[:, 1024:T], SC_C2[:, 1024:T], PSQ3[:])
        ev_acc2 = b.v(ph, vacc2, waits=[ev_mm2])
    ev_xn_stored = [(b.chan["xnst0"].h, b.chan["xnst0"].n),
                    (b.chan["xnst1"].h, b.chan["xnst1"].n)]
    stk1.close()

    def vrows2(e):
        e.tensor_scalar_mul(SC_A2[:], SC_A2[:], 1.0 / 1024.0)
        e.tensor_scalar_mul(SC_C2[:], SC_C2[:], 1.0 / 1024.0)
        e.tensor_mul(MUR[:], SC_A2[:], SC_A2[:])
        e.tensor_sub(SC_C2[:], SC_C2[:], MUR[:])
        e.tensor_scalar_add(SC_C2[:], SC_C2[:], 1e-6)
    ev = b.v(ph, vrows2, waits=[ev_acc2])

    def srows2(e):
        e.activation(SC_C2[:], SC_C2[:], AF.Ln)
        e.activation(SC_C2[:], SC_C2[:], AF.Exp, scale=-0.5)
    ev = b.s(ph, srows2, waits=[ev])

    def vc2(e):
        e.tensor_mul(SC_A2[:], SC_A2[:], SC_C2[:])
        e.tensor_scalar_mul(SC_A2[:], SC_A2[:], -1.0)
    ev_rows2 = b.v(ph, vc2, waits=[ev])

    if stop_here("xn"):
        return b

    ev_acc = None
    if skip_outer:
        ctx1b.close(); OPEN.remove(ctx1b)
        ctx1.close(); OPEN.remove(ctx1)
        ctxR.close(); OPEN.remove(ctxR)
    if not skip_outer:
        ph = b.phase("xn2_tiles")
        ev_use = [None, None]
        ev_xn2 = None
        for ft in range(NFT):
            BUF = XT1 if ft % 2 == 0 else XT2
            wv = list(ev_xn_stored) + [ev_rows2]
            if ev_use[ft % 2] is not None:
                wv.append(ev_use[ft % 2])
            ev_ld = b.dma(ph, lambda e, ft=ft, BUF=BUF: e.dma_start(
                out=BUF[:],
                in_=b.d_xn.rearrange("(ft p) t -> p ft t", p=128)[:, ft, :]),
                f"xnld{ft % 2}", waits=wv)

            def vt2(e, ft=ft, BUF=BUF):
                e.tensor_mul(BUF[:], BUF[:], SC_C2[:])
                e.tensor_add(BUF[:], BUF[:], SC_A2[:])
                e.tensor_scalar(XN2B[:, ft * T:(ft + 1) * T], BUF[:],
                                LNG[:, ft:ft + 1], LNB[:, ft:ft + 1],
                                ALU.mult, ALU.add)
            ev_xn2 = b.v(ph, vt2, waits=[ev_ld, ev_rows2, ev_cf])
            ev_use[ft % 2] = ev_xn2
        b.dump("xn2b", XN2B[:, 0:1024], [128, 1024])

        if stop_here("xn2"):
            return b

        ph = b.phase("gate")
        stkg = ExitStack()
        PSG = [b.ps(f"psg{j}", [128, 1024], F32, stack=stkg) for j in range(2)]

        def t_gate(e):
            for h in range(2):
                for c in range(2):
                    for kc in range(NFT):
                        e.matmul(PSG[h][:, c * 512:(c + 1) * 512],
                                 WENT[:, kc * 128:(kc + 1) * 128],
                                 XN2B[:, kc * T + h * 1024 + c * 512:
                                      kc * T + h * 1024 + (c + 1) * 512],
                                 start=(kc == 0), stop=(kc == NFT - 1))
        ev = b.t(ph, t_gate, waits=[ev_xn2, ev_cb])

        def s_gate(e):
            e.activation(MUR[:, 0:1024], PSG[0][:], AF.Sigmoid, bias=BENT[:, 0:1])
            e.activation(MUR[:, 1024:T], PSG[1][:], AF.Sigmoid, bias=BENT[:, 0:1])
        ev = b.s(ph, s_gate, waits=[ev, ev_cf])
        ev_gate = b.v(ph, lambda e: e.tensor_scalar_max(GB[:], MUR[:], 0.1),
                      waits=[ev])
        stkg.close()
        b.dump("gate", GB[:, 0:1024], [128, 1024])
        ctx1b.close()
        OPEN.remove(ctx1b)

        if stop_here("gate"):
            return b

        ctxATM = ExitStack()
        OPEN.append(ctxATM)
        ATM = b.sb("oATM", [128, NFT * T], BF16, stack=ctxATM)
        ctxO = ExitStack()
        OPEN.append(ctxO)
        WHP = [b.sb(f"oWHP{j}", [128, 8 * 3 * 128], BF16, stack=ctxO) for j in range(2)]
        QR = b.sb("oQR", [128, T], BF16, stack=ctxO)
        KR = b.sb("oKR", [128, T], BF16, stack=ctxO)
        VT_ = b.sb("oVT", [128, T], BF16, stack=ctxO)
        SCR = b.sb("oSCR", [128, T], BF16, stack=ctxO)
        PRB = b.sb("oPRB", [128, T], BF16, stack=ctxO)
        PRT = b.sb("oPRT", [128, T], BF16, stack=ctxO)
        EXPIN = b.sb("oEXPIN", [128, T], F32, stack=ctxO)
        stkOP = ExitStack()
        P1 = b.ps("oP1", [128, 1024], F32, stack=stkOP)
        P2 = b.ps("oP2", [128, 1024], F32, stack=stkOP)
        PT = b.ps("oPT", [128, 1024], BF16, stack=stkOP)
        PV = b.ps("oPV", [128, 512], F32, stack=stkOP)

        def whp_dma(ph, src_full, hp, waits):
            ev = None
            for r in range(3):
                src = src_full.rearrange("(kc p) (r h c) -> p kc r h c",
                                         p=128, r=3, c=128)[:, :, r, hp, :]
                dst = WHP[hp % 2][:].rearrange("p (kc r c) -> p kc r c",
                                               kc=8, r=3)[:, :, r, :]
                ev = b.dma(ph, lambda e, s=src, d=dst: e.dma_start(out=d, in_=s),
                           f"gwhp{hp % 2}", waits=waits if r == 0 else (),
                           queue="gpsimd")
            qap = WHP[hp % 2][:].rearrange("p (kc r c) -> p kc r c",
                                           kc=8, r=3)[:, :, 0, :]
            ev_s = b.v(ph, lambda e, qap=qap: e.tensor_scalar_mul(qap, qap, 1.25),
                       waits=[ev])
            return ev_s

        ph = b.phase("oattn_pre")
        ev_whp = [None] * 8
        ev_whp[0] = whp_dma(ph, i["wqkvT"], 0, [])

        g_p1 = g_p2 = g_pt = g_pv = None
        g_qr = g_kr = g_vt = g_scr = g_prb = g_prt = None
        ev_qkv_prev = None
        ev_o = None
        for hp in range(1 if upto == "ohp0" else 8):
            ph = b.phase(f"oattn_hp{hp}")
            if hp < 7 and upto != "ohp0":
                ev_whp[hp + 1] = whp_dma(
                    ph, i["wqkvT"], hp + 1,
                    [ev_qkv_prev] if (hp >= 1 and ev_qkv_prev) else [])

            ev_evac = {}
            ev_qkv_last = None
            for r, DST, bcol, gbuf in ((0, QR, 0, g_qr), (1, KR, 8, g_kr),
                                       (2, SCR, 16, g_scr)):
                for half in range(2):
                    PP = P1 if half == 0 else P2
                    wv = [ev_whp[hp], ev_xn2]
                    gg = g_p1 if half == 0 else g_p2
                    if gg is not None:
                        wv.append(gg)
                    if half == 0 and gbuf is not None:
                        wv.append(gbuf)

                    def tq(e, r=r, half=half, PP=PP, hp=hp):
                        for nt in range(2):
                            for kc in range(8):
                                e.matmul(PP[:, nt * 512:(nt + 1) * 512],
                                         WHP[hp % 2][:, kc * 384 + r * 128:
                                                     kc * 384 + (r + 1) * 128],
                                         XN2B[:, kc * T + half * 1024 + nt * 512:
                                              kc * T + half * 1024 + (nt + 1) * 512],
                                         start=(kc == 0), stop=(kc == 7))
                    ev_mm = b.t(ph, tq, waits=wv)
                    BV = BQ125 if r == 0 else BQKV
                    bc2 = hp if r == 0 else bcol + hp
                    ev_ev = b.s(ph, lambda e, DST=DST, half=half, PP=PP, BV=BV,
                                bc2=bc2: e.activation(
                        DST[:, half * 1024:(half + 1) * 1024], PP[:], AF.Identity,
                        bias=BV[:, bc2:bc2 + 1]),
                        waits=[ev_mm])
                    ev_evac[(r, half)] = ev_ev
                    if half == 0:
                        g_p1 = ev_ev
                    else:
                        g_p2 = ev_ev
                    ev_qkv_last = ev_mm
            ev_qkv_prev = ev_qkv_last
            ev_vg = b.v(ph, lambda e: e.tensor_mul(SCR[:], SCR[:], GB[:]),
                        waits=[ev_evac[(2, 1)], ev_gate] +
                              ([g_vt] if g_vt else []))
            ev_fill = None
            ev_tt_last = None
            for fill in range(2):
                wv = [ev_vg]
                if g_pt is not None:
                    wv.append(g_pt)

                def ttv(e, fill=fill):
                    for j in range(8):
                        tt = fill * 8 + j
                        e.transpose(PT[:, j * 128:(j + 1) * 128],
                                    SCR[:, tt * 128:(tt + 1) * 128], IDENT[:])
                ev_tt_last = b.t(ph, ttv, waits=wv)
                ev_fill = b.v(ph, lambda e, fill=fill: e.tensor_copy(
                    VT_[:, fill * 1024:(fill + 1) * 1024], PT[:]),
                    waits=[ev_tt_last])
                g_pt = ev_fill
            ev_vt_done = ev_fill
            g_scr = ev_tt_last

            _ndlt = int(_os.environ.get("ODLT", "2"))
            _nqt = int(_os.environ.get("OQT", "16"))
            for dlt in range(_ndlt):
                h = 2 * hp + dlt
                base = dlt * 64
                for qt in range(_nqt):
                    nchunk = qt // 4 + 1
                    chunks = []
                    for c in range(nchunk):
                        ncols = min(512, (qt + 1) * 128 - c * 512)
                        PP = (P1, P1, P2, P2)[c]
                        off = (0, 512, 0, 512)[c]
                        wv = [ev_evac[(0, 1)], ev_evac[(1, 1)], ev_vt_done]
                        gg = g_p1 if PP is P1 else g_p2
                        if gg is not None:
                            wv.append(gg)
                        has_diag = (c == nchunk - 1)
                        dcol = qt * 128 - c * 512

                        def tsc(e, base=base, qt=qt, c=c, ncols=ncols, PP=PP,
                                off=off, has_diag=has_diag, dcol=dcol):
                            e.matmul(PP[:, off:off + ncols],
                                     QR[base:base + 64, qt * 128:(qt + 1) * 128],
                                     KR[base:base + 64, c * 512:c * 512 + ncols],
                                     start=True, stop=not has_diag,
                                     tile_position=(base, 0))
                            if has_diag:
                                e.matmul(PP[:, off + dcol:off + dcol + 128],
                                         IDENT[:], MASKLO[:], start=False, stop=True)
                        ev_sc = b.t(ph, tsc, waits=wv)
                        chunks.append((c, ncols, PP, off, ev_sc))

                    r0 = 8 * (qt % 2)

                    def vmax(e, ch=tuple(chunks), r0=r0):
                        for j, (c, ncols, PP, off, _) in enumerate(ch):
                            e.reduce_max(SMALL[:, r0 + j:r0 + j + 1],
                                         PP[:, off:off + ncols], axis=AX.X)
                        e.tensor_copy(SMALL[:, r0 + 7:r0 + 8], SMALL[:, r0:r0 + 1])
                        for j in range(1, len(ch)):
                            e.tensor_max(SMALL[:, r0:r0 + 1], SMALL[:, r0:r0 + 1],
                                         SMALL[:, r0 + j:r0 + j + 1])
                        e.tensor_copy(SMALL[:, r0 + 7:r0 + 8], SMALL[:, r0:r0 + 1])
                        for j, (c, ncols, PP, off, _) in enumerate(ch):
                            e.tensor_single_scalar(
                                EXPIN[:, c * 512:c * 512 + ncols],
                                PP[:, off:off + ncols], SMALL[:, r0:r0 + 1],
                                ALU.subtract)
                    ev_mx = b.v(ph, vmax, waits=[c[4] for c in chunks] +
                                ([g_prb] if g_prb else []))

                    wv = [ev_mx]
                    if g_prb is not None:
                        wv.append(g_prb)

                    def sexp(e, ch=tuple(chunks)):
                        for j, (c, ncols, PP, off, _) in enumerate(ch):
                            e.activation(PRB[:, c * 512:c * 512 + ncols],
                                         EXPIN[:, c * 512:c * 512 + ncols], AF.Exp)
                    ev_ex = b.s(ph, sexp, waits=wv)
                    if any(c[2] is P1 for c in chunks):
                        g_p1 = ev_mx
                    if any(c[2] is P2 for c in chunks):
                        g_p2 = ev_mx

                    def vden(e, nch=nchunk, r0=r0, qt=qt):
                        e.reduce_sum(SMALL[:, r0 + 5:r0 + 6],
                                     PRB[:, 0:(qt + 1) * 128], axis=AX.X)
                        e.tensor_copy(SMALL[:, r0 + 7:r0 + 8], SMALL[:, r0:r0 + 1])
                        e.reciprocal(SMALL[:, r0 + 6:r0 + 7], SMALL[:, r0 + 5:r0 + 6])
                    ev_dn = b.v(ph, vden, waits=[ev_ex])

                    nkt = qt + 1
                    ev_fillp = None
                    for fill in range((nkt + 7) // 8):
                        j0, j1 = fill * 8, min(nkt, fill * 8 + 8)
                        wv = [ev_ex]
                        if g_pt is not None:
                            wv.append(g_pt)
                        if g_prt is not None and fill == 0:
                            wv.append(g_prt)

                        def ttp(e, j0=j0, j1=j1):
                            for j in range(j0, j1):
                                e.transpose(PT[:, (j - j0) * 128:(j - j0 + 1) * 128],
                                            PRB[:, j * 128:(j + 1) * 128], IDENT[:])
                        ev_ttp = b.t(ph, ttp, waits=wv)
                        ev_fillp = b.v(ph, lambda e, j0=j0, j1=j1: e.tensor_copy(
                            PRT[:, j0 * 128:j1 * 128], PT[:, 0:(j1 - j0) * 128]),
                            waits=[ev_ttp])
                        g_pt = ev_fillp
                    g_prb = ev_ttp

                    wv = [ev_fillp]
                    if g_pv is not None:
                        wv.append(g_pv)

                    def tpv(e, nkt=nkt, base=base):
                        for j in range(nkt):
                            e.matmul(PV[:, 0:64], PRT[:, j * 128:(j + 1) * 128],
                                     VT_[:, j * 128 + base:j * 128 + base + 64],
                                     start=(j == 0), stop=(j == nkt - 1))
                    ev_pv = b.t(ph, tpv, waits=wv)
                    ev_o = b.v(ph, lambda e, qt=qt, h=h, r0=r0: e.tensor_scalar_mul(
                        ATM[:, qt * 1024 + h * 64:qt * 1024 + h * 64 + 64],
                        PV[:, 0:64], SMALL[:, r0 + 6:r0 + 7]), waits=[ev_pv, ev_dn])
                    g_pv = ev_o
                    g_prt = ev_pv
            g_qr = g_kr = ev_sc
            g_vt = ev_pv
            if hp == 0:
                for qq in range(4):
                    b.dump(f"oatmq{qq}", ATM[:, qq * 1024:(qq + 1) * 1024],
                           [128, 1024])
                b.dump("oqr", QR[:, 0:1024], [128, 1024])
                b.dump("okr", KR[:, 0:1024], [128, 1024])
                b.dump("oprb", PRB[:, 0:1024], [128, 1024])
                b.dump("oprt", PRT[:, 0:1024], [128, 1024])
                b.dump("osmall", SMALL[:, 0:32], [128, 32])
                b.dump("oexpin", EXPIN[:, 0:1024], [128, 1024])
                b.dump("ovt", VT_[:, 0:1024], [128, 1024])
        ev_outer_done = ev_o
        b.dump("oatm", ATM[:, 0:1024], [128, 1024])
        stkOP.close()
        ctxO.close()
        OPEN.remove(ctxO)
        if stop_here("oattn") or stop_here("ohp0"):
            return b

        ph = b.phase("o_wo")
        ctxW = ExitStack()
        AFM = b.sb("oAFM", [128, NFT * T], BF16, stack=ctxW)
        WOS = b.sb("oWOS", [128, 8 * 1024], BF16, stack=ctxW)
        TSC2 = b.sb("oTSC2", [128, 1024], F32, stack=ctxW)
        stkP = ExitStack()
        PT2 = b.ps("oPT2", [128, 1024], BF16, stack=stkP)
        ev_wos = b.dma(ph, lambda e: e.dma_start(
            out=WOS[:].rearrange("p (kc c) -> p kc c", kc=8),
            in_=i["woutT"].rearrange("(kc p) c -> p kc c", p=128)),
            "gwos", queue="gpsimd")

        def epi_outer(mc, half, PWp, ev_m):
            ev1 = b.v(ph, lambda e, mc=mc, PWp=PWp: e.tensor_scalar(
                TSC2[:], PWp[:], 0.1, BOUT01[:, mc:mc + 1], ALU.mult, ALU.add),
                waits=[ev_m, ev_b01])
            ev2 = b.v(ph, lambda e, mc=mc, half=half: e.tensor_add(
                XR[:, mc * T + half * 1024:mc * T + (half + 1) * 1024],
                XR[:, mc * T + half * 1024:mc * T + (half + 1) * 1024], TSC2[:]))
            return ev2

        g_pt_box = [None]
        ev_acc_v = wo_project(b, ph, ATM, AFM, WOS, XR, IDENT, PT2,
                              [ev_outer_done, ev_wos], g_pt_box, epi_outer)
        stkP.close()

        b.dump("acc0", XR[:, 0:1024], [128, 1024])
        ph = b.phase("acc_store")
        ev_acc = b.dma(ph, lambda e: e.dma_start(
            out=b.d_acc.rearrange("(ft p) t -> p ft t", p=128),
            in_=XR[:].rearrange("p (ft t) -> p ft t", ft=NFT)), "accst",
            waits=[ev_acc_v])
        ctxW.close()
        ctxATM.close()
        ctx1.close()
        ctxR.close()
        OPEN.clear()

        if stop_here("owo"):
            return b

    XN = b.sb("XN", [128, NFT * T], F32)
    ph = b.phase("xn_reload")
    ev_xn_ld = b.dma(ph, lambda e: e.dma_start(
        out=XN[:].rearrange("p (ft t) -> p ft t", ft=NFT),
        in_=b.d_xn.rearrange("(ft p) t -> p ft t", p=128)),
        "xnfull", waits=list(ev_xn_stored))

    ctxT = ExitStack()
    TG = b.sb("TG", [1, T], F32, stack=ctxT)
    TSCR = b.sb("TSCR", [128, 4 * T], F32, stack=ctxT)
    TROWF = b.sb("TROWF", [128, T], F32, stack=ctxT)
    TROW = b.sb("TROW", [1, T], F32, stack=ctxT)
    TINV = b.sb("TINV", [1, 128], F32, stack=ctxT)
    TLNB = b.sb("TLNB", [1, 128], F32, stack=ctxT)
    TSC = b.sb("TSCc", [1, 8], F32, stack=ctxT)
    TI32 = b.sb("TI32", [128, T], I32, stack=ctxT)
    PIREG = b.sb("PIREG", [128, 6], F32, stack=ctxT)
    stkT = ExitStack()

    ph = b.phase("tables")
    ev_tg = b.dma(ph, lambda e: e.dma_start(out=TG[:], in_=i["tgridC"]), "const")

    def sc0(e):
        e.activation(TSC[:, 0:1], C4[:, 0:1], AF.Ln)
        e.activation(TSC[:, 1:2], C4[:, 3:4], AF.Arctan)
    ev = b.s(ph, sc0, waits=[ev_cf])

    def vc0(e):
        e.wait_ge(*ev_tg)
        e.tensor_scalar_mul(TSC[:, 1:2], TSC[:, 1:2], 4.0)
        e.tensor_scalar_mul(TSC[:, 4:5], TSC[:, 0:1], -1.0 / 32.0)
        e.reciprocal(TSC[:, 5:6], C4[:, 1:2])
        e.tensor_scalar_mul(TSC[:, 2:3], TSC[:, 1:2], 2.0)
        e.tensor_scalar(TROW[:, :], TG[:, :], -1024.0, 1.0 / 128.0,
                        ALU.add, ALU.mult)
        e.reciprocal(TSC[:, 3:4], TSC[:, 2:3])
    ev = b.v(ph, vc0, waits=[ev])
    def s_tinv(e):
        e.activation(TSC[:, 6:7], TSC[:, 4:5], AF.Copy)
        e.activation(TINV[:, :], JG[:, :], AF.Exp, scale=TSC[:, 4:5])
    ev = b.s(ph, s_tinv, waits=[ev])

    def vc1(e):
        e.tensor_scalar(TLNB[:, :], JG[:, :], 10.0, 128.0, ALU.mult, ALU.add)
        e.tensor_scalar_mul(TLNB[:, :], TLNB[:, :], TSC[:, 5:6])
    ev = b.v(ph, vc1, waits=[ev])
    ev = b.s(ph, lambda e: e.activation(TLNB[:, :], TLNB[:, :], AF.Ln), waits=[ev])

    stkT1 = ExitStack()
    TPIS = b.ps("TPIS", [128, 8], F32, stack=stkT1)

    def tb_pis(e):
        e.matmul(TPIS[:, 0:4], ONESK1F[0:1, :], TSC[0:1, 0:4],
                 start=True, stop=True)
    ev_t0 = b.t(ph, tb_pis, waits=[ev])

    def vpis(e):
        e.tensor_scalar_mul(PIREG[:, 0:1], TPIS[:, 1:2], 0.5)
        e.tensor_copy(PIREG[:, 1:2], TPIS[:, 3:4])
        e.tensor_copy(PIREG[:, 2:3], TPIS[:, 2:3])
        e.tensor_copy(PIREG[:, 3:4], TPIS[:, 1:2])
        e.tensor_scalar_mul(PIREG[:, 4:5], TPIS[:, 1:2], -1.0)
    ev_p0 = b.v(ph, vpis, waits=[ev_t0])
    stkT1.close()
    TPS = b.ps("TPS", [128, 2 * T], F32, stack=stkT)

    def tb0(e):
        for c in range(4):
            e.matmul(TPS[:, c * 512:(c + 1) * 512], TINV[0:1, :],
                     TG[0:1, c * 512:(c + 1) * 512], start=True, stop=True)
        for c in range(4):
            e.matmul(TPS[:, T + c * 512:T + (c + 1) * 512], TLNB[0:1, :],
                     TROW[0:1, c * 512:(c + 1) * 512], start=True, stop=True)
    ev_t = b.t(ph, tb0, waits=[ev, ev_p0])
    ev_p = ev_t

    def sscale(e):
        e.activation(TSCR[:, 2 * T:3 * T], TPS[:, T:2 * T], AF.Exp)
        e.activation(TSCR[:, 3 * T:4 * T], TPS[:, T:2 * T], AF.Exp, scale=-1.0)
    ev_scl = b.s(ph, sscale, waits=[ev_p])

    def vred(e):
        for (off, add_hpi) in ((0, False), (T, True)):
            if add_hpi:
                e.tensor_scalar_add(TSCR[:, off:off + T], TPS[:, 0:T],
                                    PIREG[:, 0:1])
                src = TSCR[:, off:off + T]
            else:
                src = TPS[:, 0:T]
            e.tensor_scalar(TROWF[:, :], src, PIREG[:, 1:2], 0.5,
                            ALU.mult, ALU.add)
            e.tensor_copy(TI32[:, :], TROWF[:, :])
            e.tensor_copy(TROWF[:, :], TI32[:, :])
            e.tensor_scalar_mul(TROWF[:, :], TROWF[:, :], PIREG[:, 2:3])
            e.tensor_sub(TSCR[:, off:off + T], src, TROWF[:, :])
            e.tensor_scalar(TROWF[:, :], TSCR[:, off:off + T], PIREG[:, 3:4],
                            None, ALU.is_gt)
            e.tensor_scalar_mul(TROWF[:, :], TROWF[:, :], PIREG[:, 2:3])
            e.tensor_sub(TSCR[:, off:off + T], TSCR[:, off:off + T], TROWF[:, :])
            e.tensor_scalar(TROWF[:, :], TSCR[:, off:off + T], PIREG[:, 4:5],
                            None, ALU.is_lt)
            e.tensor_scalar_mul(TROWF[:, :], TROWF[:, :], PIREG[:, 2:3])
            e.tensor_add(TSCR[:, off:off + T], TSCR[:, off:off + T], TROWF[:, :])
    ev_r = b.v(ph, vred, waits=[ev_scl])

    def ssin(e):
        e.activation(TSCR[:, 0:T], TSCR[:, 0:T], AF.Sin)
        e.activation(TSCR[:, T:2 * T], TSCR[:, T:2 * T], AF.Sin)
    ev_s2 = b.s(ph, ssin, waits=[ev_r])

    ev_tab_st = None
    ev_prev_dma = None
    for g, (kind, src_off) in enumerate(
            ((0, 2), (1, 2), (0, 3), (1, 3))):
        wv = [ev_s2]
        if ev_prev_dma is not None:
            wv.append(ev_prev_dma)

        def vtab(e, kind=kind, src_off=src_off):
            if kind == 0:
                e.tensor_mul(TROWF[:], TSCR[:, T:2 * T],
                             TSCR[:, src_off * T:(src_off + 1) * T])
            else:
                e.scalar_tensor_tensor(TROWF[:], TSCR[:, 0:T], SGN[:, 0:1],
                                       TSCR[:, src_off * T:(src_off + 1) * T],
                                       ALU.mult, ALU.mult)
        ev_tb = b.v(ph, vtab, waits=wv)
        ev_tab_st = b.dma(ph, lambda e, g=g: e.dma_start(
            out=b.d_tab[g * 128:(g + 1) * 128, :], in_=TROWF[:]),
            "gtabst", waits=[ev_tb], queue="gpsimd")
        ev_prev_dma = ev_tab_st
    b.dump("ttsc", TSC[0:1, 0:8], [1, 8])
    b.dump("ttinv", TINV[0:1, 0:128], [1, 128])
    b.dump("ttlnb", TLNB[0:1, 0:128], [1, 128])
    b.dump("ttrow", TROW[0:1, 0:1024], [1, 1024])
    b.dump("tpireg", PIREG[:, 0:6], [128, 6])
    for cc in range(8):
        b.dump(f"tscr{cc}", TSCR[:, cc * 1024:(cc + 1) * 1024], [128, 1024])
    stkT.close()
    ctxT.close()

    env = dict(IDENT=IDENT, PROT=PROT, MASK3=MASK3, MASKLO=MASKLO, INDR=INDR,
               ONES1B=ONES1B, FQS=FQS, FQS8=FQS8, FKS=FKS,
               FB1A=FB1A, FB1G=FB1G, FB2=FB2, FLNG=FLNG, FLNB=FLNB,
               SMALL=SMALL, ONESPB=ONESPB)
    last = ev_xn_ld
    for l in range(b.n_layers):
        last = inner_layer(b, l, XN, env, ev_tab_st, ev_xn_ld, last)

    ph = b.phase("final")
    FSC = b.sb("FSC", [128, T], F32)
    FSD = b.sb("FSD", [128, T], F32)
    ev_use = [None, None]
    for ft in range(NFT):
        BUF = FSC if ft % 2 == 0 else FSD
        wv = [] if ev_acc is None else [ev_acc]
        if ev_use[ft % 2] is not None:
            wv.append(ev_use[ft % 2])
        if ev_acc is None:
            ev_v = b.v(ph, lambda e, ft=ft, BUF=BUF: e.tensor_copy(
                BUF[:], XN[:, ft * T:(ft + 1) * T]), waits=wv + [last])
        else:
            ev_l = b.dma(ph, lambda e, ft=ft, BUF=BUF: e.dma_start(
                out=BUF[:],
                in_=b.d_acc.rearrange("(ft p) t -> p ft t", p=128)[:, ft, :]),
                f"accld{ft % 2}", waits=wv)
            ev_v = b.v(ph, lambda e, ft=ft, BUF=BUF: e.tensor_add(
                BUF[:], BUF[:], XN[:, ft * T:(ft + 1) * T]), waits=[ev_l, last])
        ev_use[ft % 2] = b.dma(ph, lambda e, ft=ft, BUF=BUF: e.dma_start(
            out=b.o_y.rearrange("(ft p) t -> p ft t", p=128)[:, ft, :],
            in_=BUF[:]), f"y{ft % 2}", waits=[ev_v])
    return b


def inner_layer(b, l, XN, env, ev_tab_st, ev_xn_ld, ev_prev):
    nc, i = b.nc, b.i
    IDENT, PROT, MASK3, MASKLO = (env["IDENT"], env["PROT"], env["MASK3"],
                                  env["MASKLO"])
    INDR, ONES1B = env["INDR"], env["ONES1B"]
    FQS8 = env["FQS8"]
    FQS, FKS = env["FQS"], env["FKS"]
    FB1A, FB1G, FB2 = env["FB1A"], env["FB1G"], env["FB2"]
    FLNG, FLNB = env["FLNG"], env["FLNB"]
    SMALL = env["SMALL"]

    ctxTAB = ExitStack()
    TAB = b.sb(f"l{l}_TAB", [128, 4 * T], BF16, stack=ctxTAB)
    ATM = b.sb(f"l{l}_ATM", [128, NFT * T], BF16, stack=ctxTAB)
    ctxL = ExitStack()
    XS = b.sb(f"l{l}_XS", [128, NFT * T], BF16, stack=ctxL)
    WHP = [b.sb(f"l{l}_WHP{j}", [128, 8 * 3 * 128], BF16, stack=ctxL)
           for j in range(2)]
    QR = b.sb(f"l{l}_QR", [128, T], BF16, stack=ctxL)
    KR = b.sb(f"l{l}_KR", [128, T], BF16, stack=ctxL)
    VT_ = b.sb(f"l{l}_VT", [128, T], BF16, stack=ctxL)
    SCR = b.sb(f"l{l}_SCR", [128, T], BF16, stack=ctxL)
    RNQ = b.sb(f"l{l}_RNQ", [128, T], F32, stack=ctxL)
    RNK = b.sb(f"l{l}_RNK", [128, T], F32, stack=ctxL)
    stkA = ExitStack()
    P1 = b.ps(f"l{l}_P1", [128, 1024], F32, stack=stkA)
    P2 = b.ps(f"l{l}_P2", [128, 1024], F32, stack=stkA)
    PSQ = b.ps(f"l{l}_PSQ", [128, 1024], F32, stack=stkA)
    PT = b.ps(f"l{l}_PT", [128, 1024], BF16, stack=stkA)
    PV = b.ps(f"l{l}_PV", [128, 512], F32, stack=stkA)

    def whp_dma(ph, hp, waits):
        ev = None
        for r in range(3):
            src = i["fWqkv"][l].rearrange("(kc p) (r h c) -> p kc r h c",
                                          p=128, r=3, c=128)[:, :, r, hp, :]
            dst = WHP[hp % 2][:].rearrange("p (kc r c) -> p kc r c",
                                           kc=8, r=3)[:, :, r, :]
            ev = b.dma(ph, lambda e, s=src, d=dst: e.dma_start(out=d, in_=s),
                       f"gwhp{hp % 2}", waits=waits if r == 0 else (),
                       queue="gpsimd")
        return ev

    ph = b.phase(f"l{l}_pre")
    ev_tab_ld = b.dma(ph, lambda e: e.dma_start(
        out=TAB[:].rearrange("p (g t) -> p g t", g=4),
        in_=b.d_tab.rearrange("(g p) t -> p g t", p=128)),
        "tabld", waits=[ev_tab_st])
    ev_whp = [None] * 8
    ev_whp[0] = whp_dma(ph, 0, [])
    ev_xs = shift_cast(b, ph, XS, XN, [ev_prev, ev_xn_ld])

    g_p1 = g_p2 = g_pt = g_pv = g_psq = None
    g_qr = g_kr = g_vt = g_scr = None
    ev_qkv_prev = None
    ev_o = None
    for hp in range(8):
        ph = b.phase(f"l{l}_hp{hp}")
        if hp < 7:
            ev_whp[hp + 1] = whp_dma(
                ph, hp + 1, [ev_qkv_prev] if (hp >= 1 and ev_qkv_prev) else [])

        ev_evac = {}
        ev_qkv_last = None
        for r, DST, gbuf in ((0, QR, g_qr), (1, KR, g_kr), (2, SCR, g_scr)):
            for half in range(2):
                PP = P1 if half == 0 else P2
                wv = [ev_whp[hp]] + list(ev_xs)
                gg = g_p1 if half == 0 else g_p2
                if gg is not None:
                    wv.append(gg)
                if half == 0 and gbuf is not None:
                    wv.append(gbuf)

                def tq(e, r=r, half=half, PP=PP, hp=hp):
                    for nt in range(2):
                        for kc in range(8):
                            e.matmul(PP[:, nt * 512:(nt + 1) * 512],
                                     WHP[hp % 2][:, kc * 384 + r * 128:
                                                 kc * 384 + (r + 1) * 128],
                                     XS[:, kc * T + half * 1024 + nt * 512:
                                        kc * T + half * 1024 + (nt + 1) * 512],
                                     start=(kc == 0), stop=(kc == 7))
                ev_mm = b.t(ph, tq, waits=wv)
                ev_ev = b.s(ph, lambda e, DST=DST, half=half, PP=PP:
                            e.activation(DST[:, half * 1024:(half + 1) * 1024],
                                         PP[:], AF.Copy), waits=[ev_mm])
                ev_evac[(r, half)] = ev_ev
                if half == 0:
                    g_p1 = ev_ev
                else:
                    g_p2 = ev_ev
                ev_qkv_last = ev_mm
        ev_qkv_prev = ev_qkv_last

        ev_fill = None
        ev_tt_last = None
        for fill in range(2):
            wv = [ev_evac[(2, fill)]]
            if g_pt is not None:
                wv.append(g_pt)
            if fill == 0 and g_vt is not None:
                wv.append(g_vt)

            def ttv(e, fill=fill):
                for j in range(8):
                    tt = fill * 8 + j
                    e.transpose(PT[:, j * 128:(j + 1) * 128],
                                SCR[:, tt * 128:(tt + 1) * 128], IDENT[:])
            ev_tt_last = b.t(ph, ttv, waits=wv)
            ev_fill = b.v(ph, lambda e, fill=fill: e.tensor_copy(
                VT_[:, fill * 1024:(fill + 1) * 1024], PT[:]),
                waits=[ev_tt_last])
            g_pt = ev_fill
        ev_vt_done = ev_fill

        ev_q2 = b.v(ph, lambda e: e.tensor_mul(SCR[:], QR[:], QR[:]),
                    waits=[ev_evac[(0, 1)], ev_tt_last])
        ev_sqmm = None
        for RN, ev_sq_in in ((RNQ, ev_q2), (RNK, None)):
            if RN is RNK:
                ev_k2 = b.v(ph, lambda e: e.tensor_mul(SCR[:], KR[:], KR[:]),
                            waits=[ev_evac[(1, 1)], ev_sqmm])
                ev_sq_in = ev_k2
            for half in range(2):
                wv = [ev_sq_in]
                if g_psq is not None:
                    wv.append(g_psq)

                def tsq(e, half=half):
                    for c in range(2):
                        e.matmul(PSQ[:, c * 512:(c + 1) * 512], INDR[:],
                                 SCR[:, half * 1024 + c * 512:
                                     half * 1024 + (c + 1) * 512],
                                 start=True, stop=True)
                ev_sqmm = b.t(ph, tsq, waits=wv)
                g_psq = b.v(ph, lambda e, half=half, RN=RN: e.tensor_copy(
                    RN[:, half * 1024:(half + 1) * 1024], PSQ[:]),
                    waits=[ev_sqmm])
        ev_rm = b.v(ph, lambda e: (e.tensor_scalar_max(RNQ[:], RNQ[:], 1e-24),
                                   e.tensor_scalar_max(RNK[:], RNK[:], 1e-24))[-1],
                    waits=[g_psq])

        def srn(e):
            e.activation(RNQ[:], RNQ[:], AF.Ln)
            e.activation(RNQ[:], RNQ[:], AF.Exp, scale=-0.5)
            e.activation(RNK[:], RNK[:], AF.Ln)
            e.activation(RNK[:], RNK[:], AF.Exp, scale=-0.5)
        ev_rc = b.s(ph, srn, waits=[ev_rm])

        ev_rot = None
        for (X, RN, SCL, cs_off, sg_off) in (
                (QR, RNQ, FQS8, 0, T), (KR, RNK, FKS, 2 * T, 3 * T)):
            ev_hat = b.v(ph, lambda e, X=X, RN=RN, SCL=SCL:
                         e.scalar_tensor_tensor(
                             X[:], X[:], SCL[:, l:l + 1], RN[:],
                             ALU.mult, ALU.mult), waits=[ev_rc])
            for half in range(2):
                PP = P1 if half == 0 else P2
                gg = g_p1 if half == 0 else g_p2
                ev_h = ev_hat

                def tu(e, X=X, half=half, PP=PP):
                    for c in range(2):
                        e.matmul(PP[:, c * 512:(c + 1) * 512], PROT[:],
                                 X[:, half * 1024 + c * 512:
                                   half * 1024 + (c + 1) * 512],
                                 start=True, stop=True)
                ev_u = b.t(ph, tu, waits=[ev_h] + ([gg] if gg else []))

                def vrot(e, X=X, half=half, cs_off=cs_off, sg_off=sg_off,
                         PP=PP):
                    e.tensor_mul(SCR[:, 0:1024],
                                 X[:, half * 1024:(half + 1) * 1024],
                                 TAB[:, cs_off + half * 1024:
                                     cs_off + (half + 1) * 1024])
                    e.tensor_mul(SCR[:, 1024:2048], PP[:],
                                 TAB[:, sg_off + half * 1024:
                                     sg_off + (half + 1) * 1024])
                    e.tensor_add(X[:, half * 1024:(half + 1) * 1024],
                                 SCR[:, 0:1024], SCR[:, 1024:2048])
                ev_rot = b.v(ph, vrot, waits=[ev_u, ev_tab_ld])
                if half == 0:
                    g_p1 = ev_rot
                else:
                    g_p2 = ev_rot
        ev_rot_done = ev_rot

        g_slot = [None, None]
        g_prb = None
        ev_pe = None
        ev_pv = None
        for qt in range(16):
            kt0 = max(0, qt - 2)
            nk = (qt - kt0 + 1) * 128
            slot = (qt % 2) * 512
            wv = [ev_rot_done, ev_vt_done]
            if g_slot[qt % 2] is not None:
                wv.append(g_slot[qt % 2])
            if g_prb is not None:
                wv.append(g_prb)

            def tsc(e, qt=qt, kt0=kt0, nk=nk, slot=slot):
                for dlt in range(2):
                    PP = P1 if dlt == 0 else P2
                    base = dlt * 64
                    e.matmul(PP[:, slot:slot + nk],
                             QR[base:base + 64, qt * 128:(qt + 1) * 128],
                             KR[base:base + 64, kt0 * 128:kt0 * 128 + nk],
                             start=True, stop=False, tile_position=(base, 0))
                    if qt == 0:
                        e.matmul(PP[:, slot:slot + 128], IDENT[:], MASKLO[:],
                                 start=False, stop=True)
                    elif qt == 1:
                        e.matmul(PP[:, slot + 128:slot + 256], IDENT[:],
                                 MASKLO[:], start=False, stop=True)
                    else:
                        e.matmul(PP[:, slot:slot + 384], IDENT[:], MASK3[:],
                                 start=False, stop=True)
            ev_sc = b.t(ph, tsc, waits=wv)

            r0 = 8 * (qt % 2)
            eoff = 1024 * (qt % 2)

            def vmax(e, slot=slot, nk=nk, r0=r0, eoff=eoff):
                e.reduce_max(SMALL[:, r0:r0 + 1], P1[:, slot:slot + nk], axis=AX.X)
                e.reduce_max(SMALL[:, r0 + 1:r0 + 2], P2[:, slot:slot + nk],
                             axis=AX.X)
                e.tensor_single_scalar(RNQ[:, eoff:eoff + nk],
                                       P1[:, slot:slot + nk],
                                       SMALL[:, r0:r0 + 1], ALU.subtract)
                e.tensor_single_scalar(RNK[:, eoff:eoff + nk],
                                       P2[:, slot:slot + nk],
                                       SMALL[:, r0 + 1:r0 + 2], ALU.subtract)
            ev_mx = b.v(ph, vmax, waits=[ev_sc] + ([g_prb] if g_prb else []))

            def sexp(e, nk=nk, eoff=eoff):
                e.activation(SCR[:, 0:nk], RNQ[:, eoff:eoff + nk], AF.Exp)
                e.activation(SCR[:, 512:512 + nk], RNK[:, eoff:eoff + nk],
                             AF.Exp)
            ev_ex = b.s(ph, sexp, waits=[ev_mx])
            g_slot[qt % 2] = ev_mx
            def vrcp(e, r0=r0, nk=nk):
                e.reduce_sum(SMALL[:, r0 + 4:r0 + 5], SCR[:, 0:nk], axis=AX.X)
                e.reduce_sum(SMALL[:, r0 + 5:r0 + 6], SCR[:, 512:512 + nk],
                             axis=AX.X)
                e.tensor_copy(SMALL[:, r0 + 3:r0 + 4], SMALL[:, r0:r0 + 1])
                e.reciprocal(SMALL[:, r0 + 6:r0 + 8], SMALL[:, r0 + 4:r0 + 6])
            ev_rcp = b.v(ph, vrcp, waits=[ev_ex])
            nkt = nk // 128
            wv = [ev_ex]
            if g_pt is not None:
                wv.append(g_pt)
            if ev_pv is not None:
                wv.append(ev_pv)

            def ttp(e, nkt=nkt):
                for dlt in range(2):
                    for j in range(nkt):
                        e.transpose(PT[:, (dlt * nkt + j) * 128:
                                       (dlt * nkt + j + 1) * 128],
                                    SCR[:, dlt * 512 + j * 128:
                                        dlt * 512 + (j + 1) * 128], IDENT[:])
            ev_tt = b.t(ph, ttp, waits=wv)
            g_prb = ev_tt
            ev_pe = b.v(ph, lambda e, nkt=nkt: e.tensor_copy(
                SCR[:, 1024:1024 + nkt * 256], PT[:, 0:nkt * 256]),
                waits=[ev_tt])
            g_pt = ev_pe
            for dlt in range(2):
                wv = [ev_pe]
                if g_pv is not None:
                    wv.append(g_pv)

                def tpv(e, dlt=dlt, nkt=nkt, kt0=kt0):
                    for j in range(nkt):
                        e.matmul(PV[:, 0:64],
                                 SCR[:, 1024 + (dlt * nkt + j) * 128:
                                     1024 + (dlt * nkt + j + 1) * 128],
                                 VT_[:, (kt0 + j) * 128 + dlt * 64:
                                     (kt0 + j) * 128 + dlt * 64 + 64],
                                 start=(j == 0), stop=(j == nkt - 1))
                ev_pv = b.t(ph, tpv, waits=wv)
                ev_o = b.v(ph, lambda e, qt=qt, hp=hp, dlt=dlt, r0=r0:
                           e.tensor_scalar_mul(
                               ATM[:, qt * 1024 + (2 * hp + dlt) * 64:
                                   qt * 1024 + (2 * hp + dlt) * 64 + 64],
                               PV[:, 0:64], SMALL[:, r0 + 6 + dlt:r0 + 7 + dlt]),
                           waits=[ev_pv, ev_rcp])
                g_pv = ev_o
        g_qr = g_kr = ev_sc
        g_vt = ev_pv
        g_scr = ev_pv
        if hp == 0:
            for cc in range(2):
                b.dump(f"l{l}qr{cc}", QR[:, cc * 1024:(cc + 1) * 1024],
                       [128, 1024])
                b.dump(f"l{l}kr{cc}", KR[:, cc * 1024:(cc + 1) * 1024],
                       [128, 1024])
                b.dump(f"l{l}vt{cc}", VT_[:, cc * 1024:(cc + 1) * 1024],
                       [128, 1024])
            for cc in range(8):
                b.dump(f"l{l}tab{cc}", TAB[:, cc * 1024:(cc + 1) * 1024],
                       [128, 1024])
            b.dump(f"l{l}xs", XS[:, 0:1024], [128, 1024])
            b.dump(f"l{l}small", SMALL[:, 0:32], [128, 32])
            for qq in range(4):
                b.dump(f"l{l}atmq{qq}", ATM[:, qq * 1024:(qq + 1) * 1024],
                       [128, 1024])
    ev_attn_done = ev_o
    b.dump(f"l{l}atm", ATM[:, 0:1024], [128, 1024])
    stkA.close()
    ctxL.close()

    ph = b.phase(f"l{l}_wo")
    ctxW = ExitStack()
    AFM = b.sb(f"l{l}_AFM", [128, NFT * T], BF16, stack=ctxW)
    WOS = b.sb(f"l{l}_WOS", [128, 8 * 1024], BF16, stack=ctxW)
    stkW = ExitStack()
    PT2 = b.ps(f"l{l}_PT2", [128, 1024], BF16, stack=stkW)
    ev_wos = b.dma(ph, lambda e: e.dma_start(
        out=WOS[:].rearrange("p (kc c) -> p kc c", kc=8),
        in_=i["fWo"][l].rearrange("(kc p) c -> p kc c", p=128)),
        "gwos", queue="gpsimd")

    def epi_inner(mc, half, PWp, ev_m):
        return b.v(ph, lambda e, mc=mc, half=half, PWp=PWp: e.tensor_add(
            XN[:, mc * T + half * 1024:mc * T + (half + 1) * 1024],
            XN[:, mc * T + half * 1024:mc * T + (half + 1) * 1024], PWp[:]),
            waits=[ev_m])

    g_pt_box = [None]
    ev_res = wo_project(b, ph, ATM, AFM, WOS, XN, IDENT, PT2,
                        [ev_attn_done, ev_wos], g_pt_box, epi_inner)
    b.dump(f"l{l}wo", XN[:, 0:1024], [128, 1024])
    for _ft in range(NFT):
        for _hf in range(2):
            b.dump(f"l{l}wo{_ft}_{_hf}",
                   XN[:, _ft * T + _hf * 1024:_ft * T + (_hf + 1) * 1024],
                   [128, 1024])
    stkW.close()
    ctxW.close()
    ctxTAB.close()

    ctxF = ExitStack()
    XSF = b.sb(f"l{l}_XSF", [128, NFT * T], BF16, stack=ctxF)
    W1S = [b.sb(f"l{l}_W1S{j}", [128, 8 * 256], BF16, stack=ctxF)
           for j in range(2)]
    W2S = [b.sb(f"l{l}_W2S{j}", [128, NBLK * 128], BF16, stack=ctxF)
           for j in range(2)]
    GSC = b.sb(f"l{l}_GSC", [128, 1024], BF16, stack=ctxF)
    ABC = b.sb(f"l{l}_ABC", [128, T], F32, stack=ctxF)
    CBC = b.sb(f"l{l}_CBC", [128, T], F32, stack=ctxF)
    XSQ2 = b.sb(f"l{l}_fxsq", [128, T], BF16, stack=ctxF)
    stkRow = ExitStack()
    FMUR = b.sb(f"l{l}_FMUR", [128, T], F32, stack=stkRow)

    ph = b.phase(f"l{l}_fpre")
    ev_xsf = shift_cast(b, ph, XSF, XN, [ev_res])
    import os as _o2
    ev_cb2 = ln_stats_bcast(b, ph, XSF, True, None, XSQ2, env["ONESPB"],
                            ABC, CBC, FMUR, 1e-5, ev_xsf[0],
                            extra=b.ev_consts,
                            raw=_o2.environ.get("STATS_RAW", "0") == "1")
    stkRow.close()
    b.dump(f"l{l}cbc", CBC[:, 0:1024], [128, 1024])
    b.dump(f"l{l}abc", ABC[:, 0:1024], [128, 1024])
    b.dump(f"l{l}abc2", ABC[:, 1024:2048], [128, 1024])
    b.dump(f"l{l}fmur", FMUR[:, 0:1024], [128, 1024])
    b.dump(f"l{l}cbc2", CBC[:, 1024:2048], [128, 1024])
    b.dump(f"l{l}xsq7", XSQ2[:, 0:1024], [128, 1024])
    for _ft in range(NFT):
        for _hf in range(2):
            b.dump(f"l{l}xsf{_ft}_{_hf}",
                   XSF[:, _ft * T + _hf * 1024:_ft * T + (_hf + 1) * 1024],
                   [128, 1024])
    HB = b.sb(f"l{l}_HB", [128, NBLK * 1024], BF16, stack=ctxF)
    ph = b.phase(f"l{l}_fnorm")
    ev_n = None
    for ft in range(NFT):
        def vn(e, ft=ft):
            e.tensor_mul(XSQ2[:], XSF[:, ft * T:(ft + 1) * T], ABC[:])
            e.tensor_add(XSQ2[:], XSQ2[:], CBC[:])
            e.tensor_scalar(XSF[:, ft * T:(ft + 1) * T], XSQ2[:],
                            FLNG[:, l * 8 + ft:l * 8 + ft + 1],
                            FLNB[:, l * 8 + ft:l * 8 + ft + 1],
                            ALU.mult, ALU.add)
        ev_n = b.v(ph, vn, waits=[ev_cb2] if ft == 0 else ())
    b.dump(f"l{l}fn", XSF[:, 0:1024], [128, 1024])
    ev_w1 = [None, None]
    ev_w1[0] = b.dma(ph, lambda e: e.dma_start(
        out=W1S[0][:].rearrange("p (kc c) -> p kc c", kc=8),
        in_=i["fW1blk"][l, 0].rearrange("(kc p) c -> p kc c", p=128)),
        "gw10", queue="gpsimd")

    g_w1buf = [ev_w1[0], None]
    g_w2buf = [None, None]
    ev_res2 = None
    for half in range(2):
        ph = b.phase(f"l{l}_ffn{half}")
        stkH = ExitStack()
        PA = [b.ps(f"l{l}h{half}_PA{j}", [128, 1024], F32, stack=stkH)
              for j in range(2)]
        PG = [b.ps(f"l{l}h{half}_PG{j}", [128, 1024], F32, stack=stkH)
              for j in range(2)]
        g_ha = [None, None]
        ev_h = None
        for blk in range(NBLK):
            par = blk % 2
            nxt = blk + 1 if blk + 1 < NBLK else (0 if half == 0 else None)
            if nxt is not None:
                wv = []
                if g_ha[(blk + 1) % 2] is not None:
                    wv.append(g_ha[(blk + 1) % 2])
                elif half == 1:
                    wv.append(g_w1last[(blk + 1) % 2])
                ev_w1[(blk + 1) % 2] = b.dma(
                    ph, lambda e, nxt=nxt, bb=(blk + 1) % 2: e.dma_start(
                        out=W1S[bb][:].rearrange("p (kc c) -> p kc c", kc=8),
                        in_=i["fW1blk"][l, nxt].rearrange("(kc p) c -> p kc c",
                                                          p=128)),
                    f"gw1{(blk + 1) % 2}", queue="gpsimd", waits=wv)
            wv = [ev_n, ev_w1[par]]
            if g_ha[par] is not None:
                wv.append(g_ha[par])

            def tmm1(e, blk=blk, par=par, half=half):
                for ab in range(2):
                    PP = (PA if ab == 0 else PG)[par]
                    for nt in range(2):
                        for kc in range(8):
                            e.matmul(PP[:, nt * 512:(nt + 1) * 512],
                                     W1S[par][:, kc * 256 + ab * 128:
                                              kc * 256 + (ab + 1) * 128],
                                     XSF[:, kc * T + half * 1024 + nt * 512:
                                         kc * T + half * 1024 + (nt + 1) * 512],
                                     start=(kc == 0), stop=(kc == 7))
            ev_m1 = b.t(ph, tmm1, waits=wv)

            def sg(e, blk=blk, par=par):
                e.activation(GSC[:], PG[par][:], AF.Gelu,
                             bias=FB1G[:, l * NBLK + blk:l * NBLK + blk + 1])
            ev_g = b.s(ph, sg, waits=[ev_m1])

            def vm(e, blk=blk, par=par):
                e.scalar_tensor_tensor(
                    HB[:, blk * 1024:(blk + 1) * 1024], PA[par][:],
                    FB1A[:, l * NBLK + blk:l * NBLK + blk + 1], GSC[:],
                    ALU.add, ALU.mult)
            ev_h = b.v(ph, vm, waits=[ev_g])
            g_ha[par] = ev_h
        g_w1last = [ev_m1, ev_m1]
        ev_h_done = ev_h
        stkH.close()

        stkH2 = ExitStack()
        PW2 = [b.ps(f"l{l}h{half}_PW2{j}", [128, 1024], F32, stack=stkH2)
               for j in range(2)]
        ev_w2 = [None, None]
        wv0 = [g_w2buf[0]] if g_w2buf[0] else []
        ev_w2[0] = b.dma(ph, lambda e: e.dma_start(
            out=W2S[0][:].rearrange("p (kc c) -> p kc c", kc=NBLK),
            in_=i["fW2"][l].rearrange("(kc p) c -> p kc c", p=128)[:, :, 0:128]),
            "gw20", queue="gpsimd", waits=wv0)
        g_pw2 = [None, None]
        for mc in range(8):
            par = mc % 2
            if mc + 1 < 8:
                wv = []
                if g_pw2[(mc + 1) % 2] is not None:
                    wv.append(g_pw2[(mc + 1) % 2])
                elif g_w2buf[(mc + 1) % 2] is not None:
                    wv.append(g_w2buf[(mc + 1) % 2])
                ev_w2[(mc + 1) % 2] = b.dma(ph, lambda e, mc=mc: e.dma_start(
                    out=W2S[(mc + 1) % 2][:].rearrange("p (kc c) -> p kc c",
                                                       kc=NBLK),
                    in_=i["fW2"][l].rearrange("(kc p) c -> p kc c", p=128)
                    [:, :, (mc + 1) * 128:(mc + 2) * 128]),
                    f"gw2{(mc + 1) % 2}", queue="gpsimd", waits=wv)
            wv = [ev_h_done, ev_w2[par]]
            if g_pw2[par] is not None:
                wv.append(g_pw2[par])

            def tw2(e, mc=mc, par=par, half=half):
                for nt in range(2):
                    for kc in range(NBLK):
                        e.matmul(PW2[par][:, nt * 512:(nt + 1) * 512],
                                 W2S[par][:, kc * 128:(kc + 1) * 128],
                                 HB[:, kc * 1024 + nt * 512:
                                    kc * 1024 + (nt + 1) * 512],
                                 start=(kc == 0), stop=(kc == NBLK - 1))
            ev_m2 = b.t(ph, tw2, waits=wv)

            def ve2(e, mc=mc, par=par, half=half):
                e.scalar_tensor_tensor(
                    XN[:, mc * T + half * 1024:mc * T + (half + 1) * 1024],
                    PW2[par][:], FB2[:, l * 8 + mc:l * 8 + mc + 1],
                    XN[:, mc * T + half * 1024:mc * T + (half + 1) * 1024],
                    ALU.add, ALU.add)
            ev_res2 = b.v(ph, ve2, waits=[ev_m2])
            g_pw2[par] = ev_res2
        g_w2buf = [ev_m2, ev_m2]
        if half == 0:
            b.dump(f"l{l}hb", HB[:, 0:1024], [128, 1024])
        stkH2.close()
    ctxF.close()
    b.dump(f"l{l}x", XN[:, 0:1024], [128, 1024])
    return ev_res2



def _pack_pp(vec, ncol):
    v = np.asarray(vec, dtype=np.float32).reshape(-1)
    assert v.size == 128 * ncol
    return np.ascontiguousarray(v.reshape(ncol, 128).T)


def host_prepare(inputs):
    f32 = np.float32
    x = np.asarray(inputs["L0_attn_x"], f32)
    wqkvT = np.ascontiguousarray(np.asarray(inputs["L0_attn_w_qkv"], f32).T)
    woutT = np.ascontiguousarray(np.asarray(inputs["L0_attn_w_out"], f32).T)
    we = np.asarray(inputs["L0_attn_w_ent"], f32).reshape(-1)
    wentR = np.empty((128, 8 * 128), f32)
    for kc in range(8):
        wentR[:, kc * 128:(kc + 1) * 128] = we[kc * 128:(kc + 1) * 128][:, None]
    bqkvP = _pack_pp(inputs["L0_attn_b_qkv"], 24)
    boutP = _pack_pp(inputs["L0_attn_b_out"], 8)
    bentR = np.broadcast_to(np.asarray(
        inputs["L0_attn_b_ent"], f32).reshape(1, 1), (128, 1)).copy()
    lngP = _pack_pp(inputs["L0_attn_ln_g"], 8)
    lnbP = _pack_pp(inputs["L0_attn_ln_b"], 8)

    fWqkv = np.ascontiguousarray(np.asarray(inputs["L0_ffn_Wqkv"], f32))
    fWo = np.ascontiguousarray(np.asarray(inputs["L0_ffn_Wo"], f32))
    W1 = np.asarray(inputs["L0_ffn_W1"], f32)
    W2 = np.asarray(inputs["L0_ffn_W2"], f32)
    b1 = np.asarray(inputs["L0_ffn_b1"], f32)
    DIP = NBLK * 128
    W1a = np.zeros((DEPTH, D, DIP), f32)
    W1g = np.zeros((DEPTH, D, DIP), f32)
    W1a[:, :, :2730] = W1[:, :, :2730]
    W1g[:, :, :2730] = W1[:, :, 2730:]
    fW1blk = np.empty((DEPTH, NBLK, D, 256), f32)
    for bk in range(NBLK):
        fW1blk[:, bk, :, 0:128] = W1a[:, :, bk * 128:(bk + 1) * 128]
        fW1blk[:, bk, :, 128:256] = W1g[:, :, bk * 128:(bk + 1) * 128]
    fW2 = np.zeros((DEPTH, DIP, D), f32)
    fW2[:, :2730, :] = W2
    b1a = np.zeros((DEPTH, DIP), f32)
    b1g = np.zeros((DEPTH, DIP), f32)
    b1a[:, :2730] = b1[:, :2730]
    b1g[:, :2730] = b1[:, 2730:]
    fb1aP = np.concatenate([_pack_pp(b1a[l], NBLK) for l in range(DEPTH)], axis=1)
    fb1gP = np.concatenate([_pack_pp(b1g[l], NBLK) for l in range(DEPTH)], axis=1)
    fb2P = np.concatenate([_pack_pp(inputs["L0_ffn_b2"][l], 8)
                           for l in range(DEPTH)], axis=1)
    flngP = np.concatenate([_pack_pp(inputs["L0_ffn_ln_g"][l], 8)
                            for l in range(DEPTH)], axis=1)
    flnbP = np.concatenate([_pack_pp(inputs["L0_ffn_ln_b"][l], 8)
                            for l in range(DEPTH)], axis=1)
    qs = np.asarray(inputs["L0_ffn_q_scale"], f32)
    ks = np.asarray(inputs["L0_ffn_k_scale"], f32)
    p64 = np.arange(128) % 64
    fqsP = np.ascontiguousarray(qs[:, p64].T)
    fksP = np.ascontiguousarray(ks[:, p64].T)

    identC = np.eye(128, dtype=f32)
    prot = np.zeros((128, 128), f32)
    for e_ in range(128):
        blk0 = (e_ // 64) * 64
        d_ = e_ % 64
        prot[blk0 + (d_ + 32 if d_ < 32 else d_ - 32), e_] = 1.0
    ii = np.arange(128)
    masklo = np.where(ii[None, :] <= ii[:, None], 0.0, NEG).astype(f32)
    maskhi = np.where(ii[None, :] >= ii[:, None], 0.0, NEG).astype(f32)
    mask3 = np.concatenate([maskhi, np.zeros((128, 128), f32), masklo], axis=1)
    indr = np.zeros((128, 128), f32)
    indr[0:64, 0:64] = 1.0
    indr[64:128, 64:128] = 1.0
    shared = dict(
        wqkvT=wqkvT, woutT=woutT, wentR=wentR, bqkvP=bqkvP, boutP=boutP,
        bentR=bentR, lngP=lngP, lnbP=lnbP, fWqkv=fWqkv, fWo=fWo, fW1blk=fW1blk,
        fW2=fW2, fb1aP=fb1aP, fb1gP=fb1gP, fb2P=fb2P, flngP=flngP, flnbP=flnbP,
        fqsP=fqsP, fksP=fksP, identC=identC, protC=prot, maskloC=masklo,
        mask3C=mask3, indrC=indr,
        ones1C=np.ones((128, 1), f32), onesK1C=np.ones((1, 128), f32),
        onesPC=np.ones((128, 128), f32),
        tgridC=np.arange(T, dtype=f32).reshape(1, T),
        jgridC=(np.arange(128) % 32).astype(f32).reshape(1, 128),
        sgnC=np.where((np.arange(128) % 64) < 32, -1.0, 1.0
                      ).astype(f32).reshape(128, 1),
        c4C=np.array([[10000.0, 448.0, 5.0, 1.0]], f32))
    per_core = []
    for bb in range(8):
        m = dict(shared)
        m["xT"] = np.ascontiguousarray(x[bb].T)
        per_core.append(m)
    return per_core


_PROG = {}


def kernel(**inputs):
    per_core = host_prepare(inputs)
    if "full" not in _PROG:
        _PROG["full"] = build().finish()
    nc = _PROG["full"]
    res = run_bass_kernel_spmd(nc, per_core, core_ids=list(range(8)))
    out = np.empty((8, T, D), dtype=np.float32)
    for bb in range(8):
        out[bb] = np.asarray(res.results[bb]["yT"]).T
    return out
